# version 03
# baseline (speedup 1.0000x reference)
import os
import numpy as np
import concourse.bass as bass
import concourse.mybir as mybir
from concourse.bass_utils import run_bass_kernel_spmd

F32 = mybir.dt.float32
BF16 = mybir.dt.bfloat16
U8 = mybir.dt.uint8
AF = mybir.ActivationFunctionType
ALU = mybir.AluOpType

D, S, NH, HD, CH, NSEQ = 1024, 2048, 16, 64, 8, 2
STRIDEM = 45
DH = 4096
EXP_BIAS = -40.0
QC = 512

ENGINES = ("sync", "tensor", "vector", "scalar", "gpsimd")
DBG = bool(int(os.environ.get("MK_DBG", "0")))


class Sem:
    __slots__ = ("h", "n")

    def __init__(self, h):
        self.h = h
        self.n = 0


class Prog:
    def __init__(self):
        self.streams = {e: [] for e in ENGINES}
        self.wm = {}

    def wait(self, eng, sem, val=None):
        if val is None:
            val = sem.n
        if val <= 0:
            return
        key = (eng, id(sem))
        if val <= self.wm.get(key, 0):
            return
        self.wm[key] = val
        self.streams[eng].append(lambda e, s=sem, v=val: e.wait_ge(s.h, v))

    def emit(self, eng, fn, inc=None, amt=1):
        if inc is not None:
            inc.n += amt
            key = (eng, id(inc))
            self.wm[key] = max(self.wm.get(key, 0), inc.n)
            self.streams[eng].append(
                lambda e, f=fn, s=inc, a=amt: f(e).then_inc(s.h, a))
        else:
            self.streams[eng].append(fn)

    def fence(self, eng, sem):
        v = sem.n
        if v <= 0:
            return
        self.streams[eng].append(lambda e, s=sem, v=v: e.wait_ge(s.h, v))

    def dma(self, eng, out, in_, inc, waits=()):
        for (s, v) in waits:
            self.wait(eng, s, v)
        self.emit(eng, lambda e, o=out, i=in_: e.dma_start(out=o, in_=i),
                  inc=inc, amt=16)


def host_layout(inp):
    f32 = np.float32
    x = np.asarray(inp["L0_attn_x"], f32)
    xT = np.ascontiguousarray(x.transpose(0, 2, 1))
    Wq = np.asarray(inp["L0_attn_Wq"], f32)
    Wk = np.asarray(inp["L0_attn_Wk"], f32)
    Wv = np.asarray(inp["L0_attn_Wv"], f32)
    Wo = np.asarray(inp["L0_attn_Wo"], f32)

    def cm(W):
        return np.ascontiguousarray(W.T.reshape(D, CH, 128).transpose(1, 0, 2))

    out = dict(
        xT=xT, wq_cm=cm(Wq), wk_cm=cm(Wk), wo_cm=cm(Wo),
        wvT=np.ascontiguousarray(Wv.T),
        w1_cm=np.ascontiguousarray(np.asarray(inp["L0_ffn_W1"], f32)
                                   .reshape(D, 32, 128).transpose(1, 0, 2)),
        w2=np.ascontiguousarray(np.asarray(inp["L0_ffn_W2"], f32)),
        wmod=np.ascontiguousarray(np.asarray(inp["L0_wmod"], f32)),
        cT=np.ascontiguousarray(np.asarray(inp["L0_c"], f32).T),
        w_r=np.ascontiguousarray(np.asarray(inp["L0_ffn_w_r"], f32).reshape(CH, 128).T),
        b_r=np.asarray(inp["L0_ffn_b_r"], f32).reshape(1, 1),
        b1m=np.ascontiguousarray(np.asarray(inp["L0_ffn_b1"], f32)
                                 .reshape(32, 128).T),
        b2m=np.ascontiguousarray(np.asarray(inp["L0_ffn_b2"], f32)
                                 .reshape(CH, 128).T),
    )
    i = np.arange(S)[:, None]
    j = np.arange(S)[None, :]
    causal = j <= i
    strided = causal & ((i - j < STRIDEM) | (j % STRIDEM == i % STRIDEM))
    fixed = causal & ((j // STRIDEM == i // STRIDEM) |
                      (j % STRIDEM >= STRIDEM - 1))
    out["mask_s"] = np.ascontiguousarray(strided.T.astype(np.uint8))
    out["mask_f"] = np.ascontiguousarray(fixed.T.astype(np.uint8))
    import ml_dtypes
    sel = np.zeros((16, 512), dtype=ml_dtypes.bfloat16)
    for r in range(8):
        sel[r, 64 * r:64 * r + 64] = 1
    out["selmat_c"] = sel
    return out


def build_program():
    nc = bass.Bass("TRN2", target_bir_lowering=False,
                   detect_race_conditions=False,
                   dynamic_dma_scratch_size=4096)

    t_x = nc.dram_tensor("xT", [NSEQ, D, S], F32, kind="ExternalInput")
    t_wq = nc.dram_tensor("wq_cm", [CH, D, 128], F32, kind="ExternalInput")
    t_wk = nc.dram_tensor("wk_cm", [CH, D, 128], F32, kind="ExternalInput")
    t_wv = nc.dram_tensor("wvT", [D, D], F32, kind="ExternalInput")
    t_wo = nc.dram_tensor("wo_cm", [CH, D, 128], F32, kind="ExternalInput")
    t_w1 = nc.dram_tensor("w1_cm", [32, D, 128], F32, kind="ExternalInput")
    t_w2 = nc.dram_tensor("w2", [DH, D], F32, kind="ExternalInput")
    t_wmod = nc.dram_tensor("wmod", [D, 6 * D], F32, kind="ExternalInput")
    t_cT = nc.dram_tensor("cT", [D, NSEQ], F32, kind="ExternalInput")
    t_wr = nc.dram_tensor("w_r", [128, CH], F32, kind="ExternalInput")
    t_br = nc.dram_tensor("b_r", [1, 1], F32, kind="ExternalInput")
    t_b1 = nc.dram_tensor("b1m", [128, 32], F32, kind="ExternalInput")
    t_b2 = nc.dram_tensor("b2m", [128, CH], F32, kind="ExternalInput")
    t_ms = nc.dram_tensor("mask_s", [S, S], U8, kind="ExternalInput")
    t_sel = nc.dram_tensor("selmat_c", [16, 512], BF16, kind="ExternalInput")
    t_mf = nc.dram_tensor("mask_f", [S, S], U8, kind="ExternalInput")
    t_y = nc.dram_tensor("y", [NSEQ, CH, 128, S], F32, kind="ExternalOutput")
    dbg = {}
    if DBG:
        for nm in ("dbg_h", "dbg_q", "dbg_k", "dbg_attn"):
            dbg[nm] = nc.dram_tensor(nm, [NSEQ, CH, 128, S], BF16,
                                     kind="ExternalOutput")
        dbg["dbg_xa"] = nc.dram_tensor("dbg_xa", [NSEQ, CH, 128, S], F32,
                                       kind="ExternalOutput")
        dbg["dbg_v"] = nc.dram_tensor("dbg_v", [NSEQ, 2, 128, 8 * 1040],
                                      BF16, kind="ExternalOutput")
        dbg["dbg_wrow"] = nc.dram_tensor("dbg_wrow", [NSEQ, 1, S], F32,
                                         kind="ExternalOutput")
        dbg["dbg_sel"] = nc.dram_tensor("dbg_sel", [NSEQ, 1, S], F32,
                                        kind="ExternalOutput")
        dbg["dbg_modv"] = nc.dram_tensor("dbg_modv", [128, 96], F32,
                                         kind="ExternalOutput")
        dbg["dbg_rz"] = nc.dram_tensor("dbg_rz", [NSEQ, 2, 16, S], BF16,
                                       kind="ExternalOutput")
        dbg["dbg_z"] = nc.dram_tensor("dbg_z", [NSEQ, 2, 16, S], BF16,
                                      kind="ExternalOutput")

    sb = nc.alloc_sbuf_tensor
    x_sb = sb("x_sb", [128, CH * S], F32)
    h_sb = sb("h_sb", [128, CH * S], BF16)
    big = sb("big", [128, 32896], BF16)
    w2res = sb("w2res", [128, 4096], BF16)
    gf32 = sb("gf32", [128, 4096], F32)
    gbf = sb("gbf", [128, 2048], BF16)
    rzbf = sb("rzbf", [16, S], BF16)
    stg = sb("stg", [65, 2 * QC], BF16)
    selmat = sb("selmat", [16, 512], BF16)
    modv = sb("modv", [128, 160], F32)
    fstage = sb("fstage", [128, 2 * D], F32)
    fbf = sb("fbf", [128, 2 * D], BF16)
    ones1 = sb("ones1", [1, 128], F32)
    onebf = sb("onebf", [128, 1], BF16)
    one32 = sb("one32", [128, 1], F32)
    b1_sb = sb("b1_sb", [128, 32], F32)
    b2_sb = sb("b2_sb", [128, CH], F32)
    br_sb = sb("br_sb", [1, 1], F32)
    wr_sb = sb("wr_sb", [128, CH], F32)
    wrbf = sb("wrbf", [128, CH], BF16)
    wcol = sb("wcol", [128, 16], F32)
    silc = sb("silc", [128, 2 * CH], F32)
    cpool = sb("cpool", [128, 4], F32)
    selu8 = sb("selu8", [128, S], U8)

    QAOF = (0, 8192)
    KOFF = 16384
    VOFF = 24576
    mask8 = w2res.ap().bitcast(U8)
    vview = big.ap()[:, VOFF:VOFF + 8 * 1040].rearrange(
        "p (h k x) -> p h k x", h=8, k=16)

    arena = nc.alloc_psum_tensor("p_arena", [128, 4096], F32)

    def pvw(off, cols, rows=128):
        return arena[0:rows, off:off + cols]

    sems = {}
    ctxs = []
    for nm in ("Ls", "Lg", "Lc", "PE", "VE", "SE", "GP", "ST"):
        cm_ = nc.semaphore(nm)
        sems[nm] = Sem(cm_.__enter__())
        ctxs.append(cm_)
    Ls, Lg, Lc = sems["Ls"], sems["Lg"], sems["Lc"]
    PE, VE, SE, GP, ST = (sems[k] for k in ("PE", "VE", "SE", "GP", "ST"))

    p = Prog()
    st = {"fstage_rel": [None, None], "fbf_rel": [0, 0]}

    p.emit("vector", lambda e: e.memset(ones1[:, :], 1.0), inc=VE)
    p.emit("vector", lambda e: e.memset(onebf[:, :], 1.0), inc=VE)
    p.emit("vector", lambda e: e.memset(one32[:, :], 1.0), inc=VE)
    p.emit("vector", lambda e: e.memset(rzbf[:, :], 1.0), inc=VE)
    p.emit("vector", lambda e: e.memset(cpool[:, 0:1], 1e-5), inc=VE)
    p.emit("vector", lambda e: e.memset(cpool[:, 1:2], EXP_BIAS), inc=VE)
    p.emit("vector", lambda e: e.memset(cpool[:, 2:3], 0.125), inc=VE)
    p.dma("sync", selmat[:, :], t_sel.ap(), Ls)
    consts_lg = Ls.n
    consts_gp = GP.n
    cpool_done = VE.n

    p.dma("sync", silc[:, 0:2 * CH],
          t_cT.ap().rearrange("(i p) b -> p i b", p=128), Ls)
    p.dma("sync", b1_sb[:, :], t_b1.ap(), Ls)
    p.dma("sync", b2_sb[:, :], t_b2.ap(), Ls)
    p.dma("sync", br_sb[:, :], t_br.ap(), Ls)
    p.dma("sync", wr_sb[:, 0:CH], t_wr.ap(), Ls)
    ld_small = Ls.n

    x_ld0 = []
    for c in range(CH):
        p.dma("scalar", x_sb[:, c * S:(c + 1) * S],
              t_x.ap()[0, 128 * c:128 * (c + 1), :], Lc)
        x_ld0.append((Lc, Lc.n))

    p.wait("scalar", Ls, ld_small)
    p.emit("scalar", lambda e: e.activation(
        out=silc[:, 0:2 * CH], in_=silc[:, 0:2 * CH], func=AF.Silu), inc=SE)
    silc_done = SE.n
    p.wait("vector", Ls, ld_small)
    p.emit("vector", lambda e: e.tensor_copy(out=wrbf[:, :],
                                             in_=wr_sb[:, 0:CH]), inc=VE)
    wrbf_done = VE.n

    for og in range(6):
        mod_evac_prev = st.get("mod_evac", 0)
        for i in range(CH):
            slot = i % 2
            p.dma("sync", fstage[:, slot * D:(slot + 1) * D],
                  t_wmod.ap()[128 * i:128 * (i + 1),
                              1024 * og:1024 * (og + 1)],
                  Ls, waits=([st["fstage_rel"][slot]]
                             if st["fstage_rel"][slot] else []))
            ldv = Ls.n
            p.wait("tensor", Ls, ldv)
            if i == 0:
                p.wait("tensor", SE, silc_done)
                p.wait("tensor", VE, mod_evac_prev)
            for oi in range(CH):
                p.emit("tensor", lambda e, sl=slot, oi=oi, i=i: e.matmul(
                    pvw(512 * oi, 2),
                    fstage[:, sl * D + 128 * oi:sl * D + 128 * (oi + 1)],
                    silc[:, 2 * i:2 * i + 2],
                    start=(i == 0), stop=(i == CH - 1)),
                    inc=(PE if oi == CH - 1 else None))
            st["fstage_rel"][slot] = (PE, PE.n)
        mm_done = PE.n
        p.wait("vector", PE, mm_done)
        for oi in range(CH):
            oc = og * CH + oi
            p.emit("vector", lambda e, oi=oi, oc=oc: e.tensor_copy(
                out=modv[:, 2 * oc:2 * oc + 2], in_=pvw(512 * oi, 2)),
                inc=(VE if oi == CH - 1 else None))
        st["mod_evac"] = VE.n
    mod_done = VE.n

    def mcol(slot, c, b):
        i0 = 16 * slot + 2 * c + b
        return modv[:, i0:i0 + 1]

    p.wait("vector", VE, mod_done)
    p.fence("vector", VE)
    p.emit("vector", lambda e: e.tensor_scalar_add(
        modv[:, 16:32], modv[:, 16:32], 1.0), inc=VE)
    p.emit("vector", lambda e: e.tensor_scalar_add(
        modv[:, 64:80], modv[:, 64:80], 1.0), inc=VE)
    for sl_ in (2, 5):
        cs = slice(16 * sl_, 16 * sl_ + 16)
        p.fence("vector", VE)
        p.emit("vector", lambda e, cs=cs: e.tensor_scalar_max(
            modv[:, 96:112], modv[:, cs], 0.25), inc=VE)
        p.emit("vector", lambda e, cs=cs: e.tensor_scalar_min(
            modv[:, 112:128], modv[:, cs], -0.25), inc=VE)
        p.emit("vector", lambda e, cs=cs: e.tensor_scalar(
            modv[:, 128:144], modv[:, cs], 0.0, None, ALU.is_ge), inc=VE)
        p.fence("vector", VE)
        p.emit("vector", lambda e: e.tensor_sub(
            modv[:, 96:112], modv[:, 96:112], modv[:, 112:128]), inc=VE)
        p.fence("vector", VE)
        p.emit("vector", lambda e: e.tensor_mul(
            modv[:, 96:112], modv[:, 96:112], modv[:, 128:144]), inc=VE)
        p.fence("vector", VE)
        p.emit("vector", lambda e, cs=cs: e.tensor_add(
            modv[:, cs], modv[:, 112:128], modv[:, 96:112]), inc=VE)
    modv_done = VE.n
    if DBG:
        p.dma("sync", dbg["dbg_modv"].ap(), modv[:, 0:96], ST,
              waits=[(VE, modv_done)])

    def stage_weight(hbm_ap, slot):
        p.dma("sync", fstage[:, slot * D:(slot + 1) * D], hbm_ap, Ls,
              waits=([st["fstage_rel"][slot]]
                     if st["fstage_rel"][slot] else []))
        ldv = Ls.n
        p.wait("vector", Ls, ldv)
        p.wait("vector", PE, st["fbf_rel"][slot])
        p.emit("vector", lambda e, sl=slot: e.tensor_copy(
            out=fbf[:, sl * D:(sl + 1) * D],
            in_=fstage[:, sl * D:(sl + 1) * D]), inc=VE)
        st["fstage_rel"][slot] = (VE, VE.n)
        return VE.n

    def cm_ap(t, oc):
        base = t.ap()[oc]
        return bass.AP(base.tensor, base.offset,
                       [[128, 128], [128 * 128, 8], [1, 128]])

    def emit_ln(b, scale_slot, shift_slot, ready):
        p.wait("tensor", SE, SE.n)
        p.wait("tensor", VE, VE.n)
        stats_prev = 0
        for c in range(CH):
            sem_c, val_c = ready[c]
            p.wait("vector", sem_c, val_c)
            p.wait("vector", PE, stats_prev)
            p.emit("vector", lambda e, c=c: e.tensor_copy(
                out=gbf[:, 0:S], in_=x_sb[:, c * S:(c + 1) * S]), inc=VE)
            xbf_v = VE.n
            p.wait("tensor", VE, xbf_v)
            for t4 in range(4):
                p.emit("tensor", lambda e, c=c, t4=t4: e.matmul(
                    pvw(512 * t4, 512, 1),
                    onebf[:, :], gbf[:, 512 * t4:512 * (t4 + 1)],
                    start=(c == 0), stop=(c == CH - 1)),
                    inc=(PE if t4 == 3 else None))
            sx_mm = PE.n
            p.wait("scalar", sem_c, val_c)
            p.wait("scalar", PE, sx_mm)
            p.emit("scalar", lambda e, c=c: e.activation(
                out=gbf[:, 0:S], in_=x_sb[:, c * S:(c + 1) * S],
                func=AF.Square), inc=SE)
            x2_v = SE.n
            p.wait("tensor", SE, x2_v)
            for t4 in range(4):
                p.emit("tensor", lambda e, c=c, t4=t4: e.matmul(
                    pvw(2048 + 512 * t4, 512, 1),
                    onebf[:, :], gbf[:, 512 * t4:512 * (t4 + 1)],
                    start=(c == 0), stop=(c == CH - 1)),
                    inc=(PE if t4 == 3 else None))
            stats_prev = PE.n
        stats_done = PE.n
        r1 = gf32[0:1, 0:S]
        r2 = gf32[0:1, S:2 * S]
        p.wait("vector", PE, stats_done)
        p.emit("vector", lambda e: e.tensor_scalar_mul(
            r1, pvw(0, S, 1), 1.0 / D), inc=VE)
        p.emit("vector", lambda e: e.tensor_scalar_mul(
            r2, pvw(2048, S, 1), 1.0 / D), inc=VE)
        p.fence("vector", VE)
        rows_v = VE.n
        p.wait("tensor", VE, rows_v)
        for t4 in range(4):
            p.emit("tensor", lambda e, t4=t4: e.matmul(
                pvw(512 * t4, 512), ones1[:, :],
                r1[:, 512 * t4:512 * (t4 + 1)],
                start=True, stop=True), inc=(PE if t4 == 3 else None))
        mub_done = PE.n
        p.wait("scalar", PE, mub_done)
        p.emit("scalar", lambda e: e.activation(out=r1, in_=r1,
                                                func=AF.Square), inc=SE)
        p.wait("vector", SE, SE.n)
        p.emit("vector", lambda e: e.tensor_sub(r2, r2, r1), inc=VE)
        p.wait("scalar", VE, VE.n)
        p.wait("scalar", VE, cpool_done)
        p.emit("scalar", lambda e: e.activation(out=r2, in_=r2, func=AF.Sqrt,
                                                bias=cpool[0:1, 0:1]), inc=SE)
        p.wait("vector", SE, SE.n)
        p.emit("vector", lambda e: e.reciprocal(out=r2, in_=r2), inc=VE)
        rstd_v = VE.n
        p.wait("tensor", VE, rstd_v)
        for t4 in range(4):
            p.emit("tensor", lambda e, t4=t4: e.matmul(
                pvw(2048 + 512 * t4, 512), ones1[:, :],
                r2[:, 512 * t4:512 * (t4 + 1)],
                start=True, stop=True), inc=(PE if t4 == 3 else None))
        rstdb_done = PE.n
        tsl = gf32[:, S:2 * S]
        p.wait("vector", PE, rstdb_done)
        p.wait("vector", VE, modv_done)
        p.wait("scalar", VE, modv_done)
        h_done = []
        for c in range(CH):
            p.emit("vector", lambda e, c=c: e.tensor_sub(
                tsl, x_sb[:, c * S:(c + 1) * S], pvw(0, S)), inc=VE)
            p.fence("vector", VE)
            p.emit("vector", lambda e, c=c: e.tensor_mul(
                h_sb[:, c * S:(c + 1) * S], tsl, pvw(2048, S)), inc=VE)
            hv = VE.n
            p.wait("scalar", VE, hv)
            p.emit("scalar", lambda e, c=c: e.activation(
                out=h_sb[:, c * S:(c + 1) * S],
                in_=h_sb[:, c * S:(c + 1) * S], func=AF.Identity,
                scale=mcol(scale_slot, c, b), bias=mcol(shift_slot, c, b)),
                inc=SE)
            h_done.append(SE.n)
        return h_done

    seq_store_done = 0
    for b in range(NSEQ):
        if b == 0:
            x_ld = x_ld0
        else:
            x_ld = []
            for c in range(CH):
                p.dma("scalar", x_sb[:, c * S:(c + 1) * S],
                      t_x.ap()[b, 128 * c:128 * (c + 1), :], Lc,
                      waits=[(ST, seq_store_done)])
                x_ld.append((Lc, Lc.n))

        h_done = emit_ln(b, 1, 0, x_ld)
        h_all = h_done[-1]
        if DBG:
            for c in range(CH):
                p.dma("sync", dbg["dbg_h"].ap()[b, c],
                      h_sb[:, c * S:(c + 1) * S], ST,
                      waits=[(SE, h_done[c])])

        out_norm = {}
        for half in range(2):
            QA = QAOF[half]
            hc0 = 4 * half
            for wt, toff, scale in ((t_wq, QA, 0.125), (t_wk, KOFF, 1.0)):
                for oci in range(4):
                    oc = hc0 + oci
                    slot = oci % 2
                    wready = stage_weight(cm_ap(wt, oc), slot)
                    for tc in range(4):
                        bank = tc
                        key = ("qk_ev", bank)
                        p.wait("tensor", SE, st.get(key, 0))
                        p.wait("tensor", VE, wready)
                        p.wait("tensor", SE, h_all)
                        for i in range(CH):
                            p.emit("tensor", lambda e, i=i, sl=slot, tc=tc,
                                   bank=bank: e.matmul(
                                       pvw(512 * bank, 512),
                                       fbf[:, sl * D + 128 * i:
                                           sl * D + 128 * (i + 1)],
                                       h_sb[:, i * S + 512 * tc:
                                            i * S + 512 * (tc + 1)],
                                       start=(i == 0), stop=(i == CH - 1)),
                                   inc=(PE if i == CH - 1 else None))
                        st["fbf_rel"][slot] = PE.n
                        mmv = PE.n
                        p.wait("scalar", PE, mmv)
                        p.wait("scalar", VE, cpool_done)
                        if scale == 1.0:
                            p.emit("scalar", lambda e, oci=oci, tc=tc,
                                   toff=toff, bank=bank: e.activation(
                                       out=big[:, toff + 2048 * oci + 512 * tc:
                                               toff + 2048 * oci +
                                               512 * (tc + 1)],
                                       in_=pvw(512 * bank, 512),
                                       func=AF.Copy), inc=SE)
                        else:
                            p.emit("scalar", lambda e, oci=oci, tc=tc,
                                   toff=toff, bank=bank: e.activation(
                                       out=big[:, toff + 2048 * oci + 512 * tc:
                                               toff + 2048 * oci +
                                               512 * (tc + 1)],
                                       in_=pvw(512 * bank, 512),
                                       func=AF.Copy, scale=cpool[:, 2:3]),
                                   inc=SE)
                        st[key] = SE.n
            qk_done = SE.n

            wv_ready = []
            for i in range(CH):
                slot = i % 2
                p.dma("sync", fstage[:, slot * D:slot * D + 512],
                      t_wv.ap()[128 * i:128 * (i + 1),
                                512 * half:512 * (half + 1)],
                      Ls, waits=([st["fstage_rel"][slot]]
                                 if st["fstage_rel"][slot] else []))
                ldv = Ls.n
                p.wait("vector", Ls, ldv)
                p.wait("vector", GP, st.get("mask_user", 0))
                p.wait("vector", PE, st.get("w2_user", 0))
                p.emit("vector", lambda e, i=i, sl=slot: e.tensor_copy(
                    out=w2res[:, 512 * i:512 * (i + 1)],
                    in_=fstage[:, sl * D:sl * D + 512]), inc=VE)
                st["fstage_rel"][slot] = (VE, VE.n)
                wv_ready.append(VE.n)
            p.wait("gpsimd", PE, PE.n)
            p.wait("gpsimd", SE, SE.n)
            p.emit("gpsimd", lambda e: e.memset(vview[:, :, :, 64:65], 1.0),
                   inc=GP)
            vones = GP.n
            for tt in range(16):
                pb = tt % 2
                key = ("v_ev", pb)
                p.wait("tensor", SE, st.get(key, 0))
                p.wait("tensor", SE, h_all)
                for i in range(CH):
                    p.wait("tensor", VE, wv_ready[i])
                    p.emit("tensor", lambda e, i=i, tt=tt, pb=pb: e.matmul(
                        pvw(2048 + 512 * pb, 512),
                        h_sb[:, i * S + 128 * tt:i * S + 128 * (tt + 1)],
                        w2res[:, 512 * i:512 * (i + 1)],
                        start=(i == 0), stop=(i == CH - 1)),
                        inc=(PE if i == CH - 1 else None))
                mmv = PE.n
                st["v_stage_mm"] = mmv
                p.wait("scalar", PE, mmv)
                p.wait("scalar", GP, vones)
                psrc = arena.ap()[:, 2048 + 512 * pb:2048 + 512 * (pb + 1)] \
                    .rearrange("p (h x) -> p h x", h=8)
                p.emit("scalar", lambda e, tt=tt, psrc=psrc: e.activation(
                    out=vview[:, :, tt:tt + 1, 0:64], in_=psrc,
                    func=AF.Copy), inc=SE)
                st[key] = SE.n
            v_done = SE.n

            if DBG:
                for c in range(4):
                    p.dma("sync", dbg["dbg_q"].ap()[b, hc0 + c],
                          big[:, QA + 2048 * c:QA + 2048 * (c + 1)], ST,
                          waits=[(SE, qk_done)])
                    p.dma("sync", dbg["dbg_k"].ap()[b, hc0 + c],
                          big[:, KOFF + 2048 * c:KOFF + 2048 * (c + 1)], ST,
                          waits=[(SE, qk_done)])
                p.dma("sync", dbg["dbg_v"].ap()[b, half],
                      big[:, VOFF:VOFF + 8 * 1040], ST,
                      waits=[(SE, v_done)])

            t_mask = t_ms if half == 0 else t_mf
            s_rel = [0, 0, 0, 0]
            p_rel = [0, 0, 0, 0]
            pv_rel = [0, 0]
            stg_rel = [0, 0]
            for qc in range(4):
                kmax = 4 * qc + 4
                for k0 in range(0, kmax, 4):
                    kn = min(4, kmax - k0)
                    mask_in = t_mask.ap()[128 * k0:128 * (k0 + kn),
                                          512 * qc:512 * (qc + 1)] \
                        .rearrange("(k p) q -> p k q", p=128)
                    mask_out = mask8[:, k0 * QC:(k0 + kn) * QC].rearrange(
                        "p (k q) -> p k q", q=QC)
                    p.dma("scalar", mask_out, mask_in, Lc,
                          waits=[(GP, st.get("mask_user", 0)),
                                 (PE, st.get("v_stage_mm", 0)),
                                 (SE, v_done)])
                mask_ld = Lc.n
                for pr in range(4):
                    for kt in range(kmax):
                        sa = 2 * (kt % 2)
                        sb_ = sa + 1
                        p.wait("tensor", SE, max(s_rel[sa], s_rel[sb_]))
                        p.wait("tensor", SE, qk_done)
                        p.emit("tensor", lambda e, c=pr, kt=kt, qc=qc,
                               sa=sa, QA=QA: e.matmul(
                                   pvw(512 * sa, 512),
                                   big[0:64, KOFF + 2048 * c + 128 * kt:
                                       KOFF + 2048 * c + 128 * (kt + 1)],
                                   big[0:64, QA + 2048 * c + 512 * qc:
                                       QA + 2048 * c + 512 * (qc + 1)],
                                   start=True, stop=True))
                        p.emit("tensor", lambda e, c=pr, kt=kt, qc=qc,
                               sb_=sb_, QA=QA: e.matmul(
                                   pvw(512 * sb_, 512),
                                   big[64:128, KOFF + 2048 * c + 128 * kt:
                                       KOFF + 2048 * c + 128 * (kt + 1)],
                                   big[64:128, QA + 2048 * c + 512 * qc:
                                       QA + 2048 * c + 512 * (qc + 1)],
                                   start=True, stop=True,
                                   tile_position=(64, 0)), inc=PE)
                        smm = PE.n
                        for hh in range(2):
                            psl = sa + hh
                            p.wait("scalar", PE, max(smm, p_rel[psl]))
                            p.wait("scalar", VE, cpool_done)
                            p.emit("scalar", lambda e, psl=psl,
                                   src=(sa + hh): e.activation(
                                       out=gbf[:, 512 * psl:512 * (psl + 1)],
                                       in_=pvw(512 * src, 512),
                                       func=AF.Exp, bias=cpool[:, 1:2]),
                                   inc=SE)
                        s_rel[sa] = SE.n
                        s_rel[sb_] = SE.n
                        expv = SE.n
                        p.wait("gpsimd", SE, expv)
                        p.wait("gpsimd", Lc, mask_ld)
                        for hh in range(2):
                            psl = sa + hh
                            p.emit("gpsimd", lambda e, psl=psl, kt=kt:
                                   e.tensor_mul(
                                       gbf[:, 512 * psl:512 * (psl + 1)],
                                       gbf[:, 512 * psl:512 * (psl + 1)],
                                       mask8[:, QC * kt:QC * (kt + 1)]),
                                   inc=GP)
                        maskv = GP.n
                        st["mask_user"] = maskv
                        p.wait("tensor", GP, maskv)
                        p.wait("tensor", SE, v_done)
                        for hh in range(2):
                            vh = 2 * pr + hh
                            psl = sa + hh
                            p.wait("tensor", VE, pv_rel[hh])
                            p.emit("tensor", lambda e, vh=vh, kt=kt, psl=psl,
                                   hh=hh, kmax=kmax: e.matmul(
                                       pvw(3072 + 512 * hh, 512, 65),
                                       big[:, VOFF + 1040 * vh + 65 * kt:
                                           VOFF + 1040 * vh + 65 * (kt + 1)],
                                       gbf[:, 512 * psl:512 * (psl + 1)],
                                       start=(kt == 0), stop=(kt == kmax - 1)),
                                   inc=(PE if hh == 1 else None))
                            p_rel[psl] = PE.n
                    st["v_user"] = PE.n
                    pvmm = PE.n
                    for hh in range(2):
                        vh = 2 * pr + hh
                        ssl = hh
                        p.wait("vector", PE, pvmm)
                        p.wait("vector", Lc, stg_rel[ssl])
                        p.emit("vector", lambda e, ssl=ssl, hh=hh:
                               e.tensor_copy(
                                   out=stg[:, QC * ssl:QC * (ssl + 1)],
                                   in_=pvw(3072 + 512 * hh, 512, 65)),
                               inc=VE)
                        pv_rel[hh] = VE.n
                        evv = VE.n
                        oa = QA + 2048 * pr + QC * qc
                        p.dma("scalar",
                              big[64 * hh:64 * (hh + 1), oa:oa + QC],
                              stg[0:64, QC * ssl:QC * (ssl + 1)], Lc,
                              waits=[(VE, evv)])
                        p.dma("scalar",
                              rzbf[vh:vh + 1, QC * qc:QC * (qc + 1)],
                              stg[64:65, QC * ssl:QC * (ssl + 1)], Lc,
                              waits=[(VE, evv),
                                     (PE, st.get("rz_mm_done", 0))])
                        stg_rel[ssl] = Lc.n
            zdma = Lc.n
            if DBG:
                p.dma("sync", dbg["dbg_z"].ap()[b, half], rzbf[:, :], ST,
                      waits=[(Lc, zdma)])
                p.wait("vector", ST, ST.n)
            p.wait("vector", Lc, zdma)

            def _recip(e):
                with nc.allow_low_precision(reason="1/Z in bf16 is fine"):
                    return e.reciprocal(out=rzbf[0:8, :], in_=rzbf[0:8, :])
            p.emit("vector", _recip, inc=VE)
            rz_done = VE.n
            if DBG:
                p.dma("sync", dbg["dbg_rz"].ap()[b, half], rzbf[:, :], ST,
                      waits=[(VE, rz_done)])
            for ci in range(4):
                for t4 in range(4):
                    pb2 = t4 % 2
                    p.wait("tensor", VE, max(rz_done,
                                             st.get(("rz_ev", pb2), 0)))
                    p.wait("tensor", Ls, consts_lg)
                    p.emit("tensor", lambda e, ci=ci, t4=t4, pb2=pb2:
                           e.matmul(
                               pvw(2048 + 512 * pb2, 512),
                               selmat[:, 128 * ci:128 * (ci + 1)],
                               rzbf[:, 512 * t4:512 * (t4 + 1)],
                               start=True, stop=True), inc=PE)
                    mmz = PE.n
                    p.wait("vector", PE, mmz)
                    p.emit("vector", lambda e, ci=ci, t4=t4, pb2=pb2, QA=QA:
                           e.tensor_mul(
                               big[:, QA + 2048 * ci + 512 * t4:
                                   QA + 2048 * ci + 512 * (t4 + 1)],
                               big[:, QA + 2048 * ci + 512 * t4:
                                   QA + 2048 * ci + 512 * (t4 + 1)],
                               pvw(2048 + 512 * pb2, 512)), inc=VE)
                    st[("rz_ev", pb2)] = VE.n
                out_norm[hc0 + ci] = VE.n
            st["rz_mm_done"] = PE.n
            if DBG:
                for ci in range(4):
                    p.dma("sync", dbg["dbg_attn"].ap()[b, hc0 + ci],
                          big[:, QA + 2048 * ci:QA + 2048 * (ci + 1)], ST,
                          waits=[(VE, out_norm[hc0 + ci])])

        p.wait("tensor", SE, SE.n)
        SAFE = bool(int(os.environ.get("MK_SAFE_WO", "0")))
        for dc in range(CH):
            slot = dc % 2
            wready = stage_weight(cm_ap(t_wo, dc), slot)
            for tc in range(4):
                bank = tc
                if SAFE:
                    p.wait("tensor", VE, VE.n)
                    p.wait("tensor", SE, SE.n)
                    p.wait("tensor", Ls, Ls.n)
                p.wait("tensor", VE, wready)
                p.wait("tensor", VE, st.get(("wo_ev", bank), 0))
                for fi in range(CH):
                    p.wait("tensor", VE, out_norm[fi])
                    src = QAOF[fi // 4] + 2048 * (fi % 4)
                    p.emit("tensor", lambda e, fi=fi, tc=tc, sl=slot,
                           bank=bank, src=src: e.matmul(
                               pvw(512 * bank, 512),
                               fbf[:, sl * D + 128 * fi:
                                   sl * D + 128 * (fi + 1)],
                               big[:, src + 512 * tc:src + 512 * (tc + 1)],
                               start=(fi == 0), stop=(fi == CH - 1)),
                           inc=(PE if fi == CH - 1 else None))
                st["fbf_rel"][slot] = PE.n
                mmv = PE.n
                p.wait("vector", PE, mmv)
                p.wait("vector", VE, modv_done)
                p.emit("vector", lambda e, dc=dc, tc=tc, bank=bank, b=b:
                       e.scalar_tensor_tensor(
                           out=x_sb[:, dc * S + 512 * tc:
                                    dc * S + 512 * (tc + 1)],
                           in0=pvw(512 * bank, 512),
                           scalar=mcol(2, dc, b),
                           in1=x_sb[:, dc * S + 512 * tc:
                                    dc * S + 512 * (tc + 1)],
                           op0=ALU.mult, op1=ALU.add), inc=VE)
                st[("wo_ev", bank)] = VE.n
                if SAFE:
                    p.wait("vector", PE, PE.n)
        xattn_done = VE.n
        if DBG:
            for c in range(CH):
                p.dma("sync", dbg["dbg_xa"].ap()[b, c],
                      x_sb[:, c * S:(c + 1) * S], ST,
                      waits=[(VE, xattn_done)])

        h2_done = emit_ln(b, 4, 3, [(VE, xattn_done)] * CH)
        h2_all = h2_done[-1]

        p.wait("tensor", VE, wrbf_done)
        p.wait("tensor", SE, h2_all)
        for c in range(CH):
            for t4 in range(4):
                p.emit("tensor", lambda e, c=c, t4=t4: e.matmul(
                    pvw(2048 + 512 * t4, 512, 1),
                    wrbf[:, c:c + 1],
                    h_sb[:, c * S + 512 * t4:c * S + 512 * (t4 + 1)],
                    start=(c == 0), stop=(c == CH - 1)),
                    inc=(PE if (c == CH - 1 and t4 == 3) else None))
        rmm = PE.n
        wrow = gf32[0:1, S:2 * S]
        p.wait("scalar", PE, rmm)
        p.wait("scalar", Ls, ld_small)
        p.emit("scalar", lambda e: e.activation(
            out=wrow, in_=pvw(2048, S, 1), func=AF.Identity,
            bias=br_sb[0:1, 0:1]), inc=SE)
        wrow_done = SE.n
        if DBG:
            p.dma("sync", dbg["dbg_wrow"].ap()[b], wrow, ST,
                  waits=[(SE, wrow_done)])
        p.wait("tensor", SE, wrow_done)
        for t4 in range(4):
            p.emit("tensor", lambda e, t4=t4: e.matmul(
                pvw(2048 + 512 * t4, 512), ones1[:, :],
                wrow[:, 512 * t4:512 * (t4 + 1)], start=True, stop=True),
                inc=(PE if t4 == 3 else None))
        wbc_mm = PE.n
        p.dma("sync", wcol[:, 0:16],
              wrow.rearrange("a (q c) -> a q c", c=16), Ls,
              waits=[(SE, wrow_done)])
        wcol_done = Ls.n
        p.emit("vector", lambda e: e.memset(gf32[:, 0:S], 0.0), inc=VE)
        p.wait("vector", Ls, wcol_done)
        p.wait("vector", PE, wbc_mm)
        cmpb = gf32[:, S:2 * S]
        for sc in range(16):
            p.fence("vector", VE)
            p.emit("vector", lambda e, sc=sc: e.tensor_scalar(
                cmpb, pvw(2048, S), wcol[:, sc:sc + 1], None, ALU.is_le),
                inc=VE)
            p.fence("vector", VE)
            p.emit("vector", lambda e: e.tensor_add(
                gf32[:, 0:S], gf32[:, 0:S], cmpb), inc=VE)
        p.fence("vector", VE)
        acc_done = VE.n
        p.wait("tensor", VE, acc_done)
        for t4 in range(4):
            p.emit("tensor", lambda e, t4=t4: e.matmul(
                pvw(2048 + 512 * t4, 512, 1), one32[:, :],
                gf32[:, 512 * t4:512 * (t4 + 1)],
                start=True, stop=True), inc=(PE if t4 == 3 else None))
        ct_mm = PE.n
        selrow = gf32[0:1, S:2 * S]
        p.wait("vector", PE, ct_mm)
        p.emit("vector", lambda e: e.tensor_scalar(
            selrow, pvw(2048, S, 1), 1023.5, None, ALU.is_le), inc=VE)
        sel_done = VE.n
        if DBG:
            p.dma("sync", dbg["dbg_sel"].ap()[b], selrow, ST,
                  waits=[(VE, sel_done)])
        p.wait("tensor", VE, sel_done)
        for t4 in range(4):
            p.emit("tensor", lambda e, t4=t4: e.matmul(
                pvw(2048 + 512 * t4, 512), ones1[:, :],
                selrow[:, 512 * t4:512 * (t4 + 1)], start=True, stop=True),
                inc=(PE if t4 == 3 else None))
        selb_mm = PE.n
        p.wait("vector", PE, selb_mm)
        p.emit("vector", lambda e: e.tensor_copy(
            out=selu8[:, :], in_=pvw(2048, S)), inc=VE)
        selbf_done = VE.n

        for tr in range(2):
            TB = 1024 * tr
            for oc in range(32):
                slot = oc % 2
                wready = stage_weight(cm_ap(t_w1, oc), slot)
                pb = oc % 2
                key = ("w1_ev", pb)
                p.wait("tensor", SE, st.get(key, 0))
                p.wait("tensor", VE, wready)
                p.wait("tensor", SE, h2_all)
                for i in range(CH):
                    for nh in range(2):
                        p.emit("tensor", lambda e, i=i, sl=slot, nh=nh,
                               pb=pb, TB=TB: e.matmul(
                                   pvw(1024 * pb + 512 * nh, 512),
                                   fbf[:, sl * D + 128 * i:
                                       sl * D + 128 * (i + 1)],
                                   h_sb[:, i * S + TB + 512 * nh:
                                        i * S + TB + 512 * (nh + 1)],
                                   start=(i == 0), stop=(i == CH - 1)),
                               inc=(PE if (i == CH - 1 and nh == 1)
                                    else None))
                st["fbf_rel"][slot] = PE.n
                w1mm = PE.n
                p.wait("scalar", PE, w1mm)
                p.wait("scalar", Ls, ld_small)
                p.emit("scalar", lambda e, oc=oc, pb=pb: e.activation(
                    out=big[:, 1024 * oc:1024 * (oc + 1)],
                    in_=pvw(1024 * pb, 1024), func=AF.Gelu,
                    bias=b1_sb[:, oc:oc + 1]), inc=SE)
                st[key] = SE.n
            hid_done = SE.n

            for dhalf in range(2):
                p.wait("tensor", SE, hid_done)
                for dd in range(4):
                    p.wait("tensor", SE, st.get(("w2_ev", dd), 0))
                for hc in range(32):
                    slot = hc % 2
                    wready = stage_weight(t_w2.ap()[128 * hc:128 * (hc + 1),
                                                    1024 * 0:1024 * 1], slot)
                    p.wait("tensor", VE, wready)
                    for dd in range(4):
                        dcp = 4 * dhalf + dd
                        for nh in range(2):
                            p.emit("tensor", lambda e, hc=hc, dcp=dcp,
                                   nh=nh, dd=dd, sl=slot: e.matmul(
                                       pvw(1024 * dd + 512 * nh, 512),
                                       fbf[:, sl * D +
                                           128 * (dcp % CH):
                                           sl * D + 128 * (dcp % CH) + 128],
                                       big[:, 1024 * hc + 512 * nh:
                                           1024 * hc + 512 * (nh + 1)],
                                       start=(hc == 0), stop=(hc == 31)),
                                   inc=(PE if (dd == 3 and nh == 1)
                                        else None))
                    st["fbf_rel"][slot] = PE.n
                w2mm = PE.n
                t1 = gf32[:, S:S + 1024]
                t3 = gf32[:, S + 1024:S + 2048]
                for dd in range(4):
                    dcp = 4 * dhalf + dd
                    p.wait("scalar", PE, w2mm)
                    p.wait("scalar", VE, st.get("epi_t3", 0))
                    p.emit("scalar", lambda e, dcp=dcp, dd=dd: e.activation(
                        out=t1, in_=pvw(1024 * dd, 1024),
                        func=AF.Identity,
                        bias=b2_sb[:, dcp:dcp + 1]), inc=SE)
                    st[("w2_ev", dd)] = SE.n
                    t1v = SE.n
                    p.wait("vector", SE, t1v)
                    p.wait("vector", VE, selbf_done)
                    p.emit("vector", lambda e, dcp=dcp, TB=TB: e.select(
                        out=t3, mask=selu8[:, TB:TB + 1024], on_true=t1,
                        on_false=h_sb[:, dcp * S + TB:dcp * S + TB + 1024]),
                        inc=VE)
                    p.fence("vector", VE)
                    p.emit("vector", lambda e, dcp=dcp, TB=TB, b=b:
                           e.scalar_tensor_tensor(
                               out=x_sb[:, dcp * S + TB:
                                        dcp * S + TB + 1024],
                               in0=t3, scalar=mcol(5, dcp, b),
                               in1=x_sb[:, dcp * S + TB:
                                        dcp * S + TB + 1024],
                               op0=ALU.mult, op1=ALU.add), inc=VE)
                    st["epi_t3"] = VE.n
                    xfin = VE.n
                    p.dma("sync", t_y.ap()[b, dcp][:, TB:TB + 1024],
                          x_sb[:, dcp * S + TB:dcp * S + TB + 1024], ST,
                          waits=[(VE, xfin)])
        seq_store_done = ST.n

    p.wait("sync", ST, ST.n)
    p.wait("sync", Lc, Lc.n)
    p.wait("sync", Lg, Lg.n)

    with nc.Block() as block:
        for eng_name in ENGINES:
            def make(nm):
                def runner(eng):
                    for fn in p.streams[nm]:
                        fn(eng)
                return runner
            getattr(block, eng_name)(make(eng_name))
    for cm_ in ctxs:
        cm_.__exit__(None, None, None)
    return nc


def kernel(**inputs):
    lay = host_layout(inputs)
    nc = build_program()
    in_maps = []
    for core in range(8):
        m = {}
        for k, v in lay.items():
            if k == "xT":
                m[k] = np.ascontiguousarray(v[2 * core:2 * core + 2])
            elif k == "cT":
                m[k] = np.ascontiguousarray(v[:, 2 * core:2 * core + 2])
            else:
                m[k] = v
        in_maps.append(m)
    res = run_bass_kernel_spmd(nc, in_maps, core_ids=list(range(8)))
    outs = []
    for core in range(8):
        y = np.asarray(res.results[core]["y"])
        outs.append(y.transpose(0, 3, 1, 2).reshape(NSEQ, S, D))
    return np.concatenate(outs, axis=0).astype(np.float32)


# version 07
# speedup vs baseline: 1.0031x; 1.0031x over previous
import os
import numpy as np
import concourse.bass as bass
import concourse.mybir as mybir
from concourse.bass_utils import run_bass_kernel_spmd

F32 = mybir.dt.float32
BF16 = mybir.dt.bfloat16
U8 = mybir.dt.uint8
AF = mybir.ActivationFunctionType
ALU = mybir.AluOpType

D, S, NH, HD, CH, NSEQ = 1024, 2048, 16, 64, 8, 2
STRIDEM = 45
DH = 4096
EXP_BIAS = -40.0
QC = 512

ENGINES = ("sync", "tensor", "vector", "scalar", "gpsimd")
DBG = bool(int(os.environ.get("MK_DBG", "0")))


class Sem:
    __slots__ = ("h", "n")

    def __init__(self, h):
        self.h = h
        self.n = 0


class Prog:
    def __init__(self):
        self.streams = {e: [] for e in ENGINES}
        self.wm = {}

    def wait(self, eng, sem, val=None):
        if val is None:
            val = sem.n
        if val <= 0:
            return
        key = (eng, id(sem))
        if val <= self.wm.get(key, 0):
            return
        self.wm[key] = val
        self.streams[eng].append(lambda e, s=sem, v=val: e.wait_ge(s.h, v))

    def emit(self, eng, fn, inc=None, amt=1):
        if inc is not None:
            inc.n += amt
            key = (eng, id(inc))
            self.wm[key] = max(self.wm.get(key, 0), inc.n)
            self.streams[eng].append(
                lambda e, f=fn, s=inc, a=amt: f(e).then_inc(s.h, a))
        else:
            self.streams[eng].append(fn)

    def fence(self, eng, sem):
        v = sem.n
        if v <= 0:
            return
        self.streams[eng].append(lambda e, s=sem, v=v: e.wait_ge(s.h, v))

    def dma(self, eng, out, in_, inc, waits=()):
        for (s, v) in waits:
            self.wait(eng, s, v)
        self.emit(eng, lambda e, o=out, i=in_: e.dma_start(out=o, in_=i),
                  inc=inc, amt=16)


def host_layout(inp):
    f32 = np.float32
    x = np.asarray(inp["L0_attn_x"], f32)
    xT = np.ascontiguousarray(x.transpose(0, 2, 1))
    Wq = np.asarray(inp["L0_attn_Wq"], f32)
    Wk = np.asarray(inp["L0_attn_Wk"], f32)
    Wv = np.asarray(inp["L0_attn_Wv"], f32)
    Wo = np.asarray(inp["L0_attn_Wo"], f32)

    def cm(W):
        return np.ascontiguousarray(W.T.reshape(D, CH, 128).transpose(1, 0, 2))

    out = dict(
        xT=xT, wq_cm=cm(Wq), wk_cm=cm(Wk), wo_cm=cm(Wo),
        wvT=np.ascontiguousarray(Wv.T),
        w1_cm=np.ascontiguousarray(np.asarray(inp["L0_ffn_W1"], f32)
                                   .reshape(D, 32, 128).transpose(1, 0, 2)),
        w2=np.ascontiguousarray(np.asarray(inp["L0_ffn_W2"], f32)),
        wmod=np.ascontiguousarray(np.asarray(inp["L0_wmod"], f32)),
        cT=np.ascontiguousarray(np.asarray(inp["L0_c"], f32).T),
        w_r=np.ascontiguousarray(np.asarray(inp["L0_ffn_w_r"], f32).reshape(CH, 128).T),
        b_r=np.asarray(inp["L0_ffn_b_r"], f32).reshape(1, 1),
        b1m=np.ascontiguousarray(np.asarray(inp["L0_ffn_b1"], f32)
                                 .reshape(32, 128).T),
        b2m=np.ascontiguousarray(np.asarray(inp["L0_ffn_b2"], f32)
                                 .reshape(CH, 128).T),
    )
    i = np.arange(S)[:, None]
    j = np.arange(S)[None, :]
    causal = j <= i
    strided = causal & ((i - j < STRIDEM) | (j % STRIDEM == i % STRIDEM))
    fixed = causal & ((j // STRIDEM == i // STRIDEM) |
                      (j % STRIDEM >= STRIDEM - 1))
    out["mask_s"] = np.ascontiguousarray(strided.T.astype(np.uint8))
    out["mask_f"] = np.ascontiguousarray(fixed.T.astype(np.uint8))
    import ml_dtypes
    out["id128"] = np.eye(128, dtype=ml_dtypes.bfloat16)
    sel = np.zeros((16, 512), dtype=ml_dtypes.bfloat16)
    for r in range(8):
        sel[r, 64 * r:64 * r + 64] = 1
    out["selmat_c"] = sel
    return out


def build_program():
    nc = bass.Bass("TRN2", target_bir_lowering=False,
                   detect_race_conditions=False,
                   dynamic_dma_scratch_size=4096)

    t_x = nc.dram_tensor("xT", [NSEQ, D, S], F32, kind="ExternalInput")
    t_wq = nc.dram_tensor("wq_cm", [CH, D, 128], F32, kind="ExternalInput")
    t_wk = nc.dram_tensor("wk_cm", [CH, D, 128], F32, kind="ExternalInput")
    t_wv = nc.dram_tensor("wvT", [D, D], F32, kind="ExternalInput")
    t_wo = nc.dram_tensor("wo_cm", [CH, D, 128], F32, kind="ExternalInput")
    t_w1 = nc.dram_tensor("w1_cm", [32, D, 128], F32, kind="ExternalInput")
    t_w2 = nc.dram_tensor("w2", [DH, D], F32, kind="ExternalInput")
    t_wmod = nc.dram_tensor("wmod", [D, 6 * D], F32, kind="ExternalInput")
    t_cT = nc.dram_tensor("cT", [D, NSEQ], F32, kind="ExternalInput")
    t_wr = nc.dram_tensor("w_r", [128, CH], F32, kind="ExternalInput")
    t_br = nc.dram_tensor("b_r", [1, 1], F32, kind="ExternalInput")
    t_b1 = nc.dram_tensor("b1m", [128, 32], F32, kind="ExternalInput")
    t_b2 = nc.dram_tensor("b2m", [128, CH], F32, kind="ExternalInput")
    t_ms = nc.dram_tensor("mask_s", [S, S], U8, kind="ExternalInput")
    t_sel = nc.dram_tensor("selmat_c", [16, 512], BF16, kind="ExternalInput")
    t_id = nc.dram_tensor("id128", [128, 128], BF16, kind="ExternalInput")
    t_mf = nc.dram_tensor("mask_f", [S, S], U8, kind="ExternalInput")
    t_y = nc.dram_tensor("y", [NSEQ, CH, 128, S], F32, kind="ExternalOutput")
    dbg = {}
    if DBG:
        for nm in ("dbg_h", "dbg_q", "dbg_k", "dbg_attn"):
            dbg[nm] = nc.dram_tensor(nm, [NSEQ, CH, 128, S], BF16,
                                     kind="ExternalOutput")
        dbg["dbg_xa"] = nc.dram_tensor("dbg_xa", [NSEQ, CH, 128, S], F32,
                                       kind="ExternalOutput")
        dbg["dbg_v"] = nc.dram_tensor("dbg_v", [NSEQ, 2, 128, 8 * 1040],
                                      BF16, kind="ExternalOutput")
        dbg["dbg_wrow"] = nc.dram_tensor("dbg_wrow", [NSEQ, 1, S], F32,
                                         kind="ExternalOutput")
        dbg["dbg_sel"] = nc.dram_tensor("dbg_sel", [NSEQ, 1, S], F32,
                                        kind="ExternalOutput")
        dbg["dbg_modv"] = nc.dram_tensor("dbg_modv", [128, 96], F32,
                                         kind="ExternalOutput")
        dbg["dbg_rz"] = nc.dram_tensor("dbg_rz", [NSEQ, 2, 16, S], BF16,
                                       kind="ExternalOutput")
        dbg["dbg_selu8"] = nc.dram_tensor("dbg_selu8", [NSEQ, 128, S], U8,
                                          kind="ExternalOutput")
        dbg["dbg_acc"] = nc.dram_tensor("dbg_acc", [NSEQ, 128, S], F32,
                                        kind="ExternalOutput")
        dbg["dbg_wcol"] = nc.dram_tensor("dbg_wcol", [NSEQ, 128, 16], F32,
                                         kind="ExternalOutput")
        dbg["dbg_h2"] = nc.dram_tensor("dbg_h2", [NSEQ, CH, 128, S], BF16,
                                       kind="ExternalOutput")
        dbg["dbg_wrow2"] = nc.dram_tensor("dbg_wrow2", [NSEQ, 1, S], BF16,
                                          kind="ExternalOutput")
        dbg["dbg_wrbf"] = nc.dram_tensor("dbg_wrbf", [128, CH], BF16,
                                         kind="ExternalOutput")
        dbg["dbg_wrsb"] = nc.dram_tensor("dbg_wrsb", [128, CH], F32,
                                         kind="ExternalOutput")
        dbg["dbg_wbc"] = nc.dram_tensor("dbg_wbc", [NSEQ, 128, S], F32,
                                        kind="ExternalOutput")
        dbg["dbg_z"] = nc.dram_tensor("dbg_z", [NSEQ, 2, 16, S], BF16,
                                      kind="ExternalOutput")

    sb = nc.alloc_sbuf_tensor
    x_sb = sb("x_sb", [128, CH * S], F32)
    h_sb = sb("h_sb", [128, CH * S], BF16)
    big = sb("big", [128, 32896], BF16)
    w2res = sb("w2res", [128, 4096], BF16)
    gf32 = sb("gf32", [128, 4096], F32)
    gbf = sb("gbf", [128, 2048], BF16)
    rzbf = sb("rzbf", [16, S], BF16)
    stg = sb("stg", [65, 2 * QC], BF16)
    selmat = sb("selmat", [16, 512], BF16)
    id128 = sb("id128s", [128, 128], BF16)
    modv = sb("modv", [128, 160], F32)
    fstage = sb("fstage", [128, 2 * D], F32)
    fbf = sb("fbf", [128, 2 * D], BF16)
    ones1 = sb("ones1", [1, 128], F32)
    onebf = sb("onebf", [128, 1], BF16)
    one32 = sb("one32", [128, 1], F32)
    b1_sb = sb("b1_sb", [128, 32], F32)
    b2_sb = sb("b2_sb", [128, CH], F32)
    br_sb = sb("br_sb", [1, 1], F32)
    wr_sb = sb("wr_sb", [128, CH], F32)
    wrbf = sb("wrbf", [128, CH], BF16)
    wrep = sb("wrep", [128, CH * 128], BF16)
    wcol = sb("wcol", [128, 16], F32)
    silc = sb("silc", [128, 2 * CH], F32)
    cpool = sb("cpool", [128, 6], F32)

    QAOF = (0, 8192)
    KOFF = 16384
    VOFF = 24576
    masku8 = sb("masku8", [128, 8192], U8)
    vview = big.ap()[:, VOFF:VOFF + 8 * 1040].rearrange(
        "p (h k x) -> p h k x", h=8, k=16)

    selu8 = masku8
    arena = nc.alloc_psum_tensor("p_arena", [128, 4096], F32)

    def pvw(off, cols, rows=128):
        return arena[0:rows, off:off + cols]

    sems = {}
    ctxs = []
    for nm in ("Ls", "Lg", "Lc", "PE", "VE", "SE", "GP", "ST"):
        cm_ = nc.semaphore(nm)
        sems[nm] = Sem(cm_.__enter__())
        ctxs.append(cm_)
    Ls, Lg, Lc = sems["Ls"], sems["Lg"], sems["Lc"]
    PE, VE, SE, GP, ST = (sems[k] for k in ("PE", "VE", "SE", "GP", "ST"))

    p = Prog()
    st = {"fstage_rel": [None, None], "fbf_rel": [0, 0]}

    def vpad():
        p.emit("vector", lambda e: e.memset(cpool[:, 4:5], 0.0), inc=VE)

    p.emit("vector", lambda e: e.memset(ones1[:, :], 1.0), inc=VE)
    p.emit("vector", lambda e: e.memset(onebf[:, :], 1.0), inc=VE)
    p.emit("vector", lambda e: e.memset(one32[:, :], 1.0), inc=VE)
    p.emit("vector", lambda e: e.memset(rzbf[:, :], 1.0), inc=VE)
    p.emit("vector", lambda e: e.memset(cpool[:, 0:1], 1e-5), inc=VE)
    p.emit("vector", lambda e: e.memset(cpool[:, 1:2], EXP_BIAS), inc=VE)
    p.emit("vector", lambda e: e.memset(cpool[:, 2:3], 0.125), inc=VE)
    p.dma("sync", selmat[:, :], t_sel.ap(), Ls)
    p.dma("sync", id128[:, :], t_id.ap(), Ls)
    consts_lg = Ls.n
    consts_gp = GP.n
    cpool_done = VE.n

    p.dma("sync", silc[:, 0:2 * CH],
          t_cT.ap().rearrange("(i p) b -> p i b", p=128), Ls)
    p.dma("sync", b1_sb[:, :], t_b1.ap(), Ls)
    p.dma("sync", b2_sb[:, :], t_b2.ap(), Ls)
    p.dma("sync", br_sb[:, :], t_br.ap(), Ls)
    p.dma("sync", wr_sb[:, 0:CH], t_wr.ap(), Ls)
    ld_small = Ls.n

    x_ld0 = []
    for c in range(CH):
        p.dma("scalar", x_sb[:, c * S:(c + 1) * S],
              t_x.ap()[0, 128 * c:128 * (c + 1), :], Lc)
        x_ld0.append((Lc, Lc.n))

    p.wait("scalar", Ls, ld_small)
    p.emit("scalar", lambda e: e.activation(
        out=silc[:, 0:2 * CH], in_=silc[:, 0:2 * CH], func=AF.Silu), inc=SE)
    p.wait("scalar", VE, cpool_done)
    p.emit("scalar", lambda e: e.activation(out=cpool[:, 3:4],
                                            in_=cpool[:, 0:1],
                                            func=AF.Identity), inc=SE)
    silc_done = SE.n
    p.wait("vector", Ls, ld_small)
    p.emit("vector", lambda e: e.tensor_copy(out=wrbf[:, :],
                                             in_=wr_sb[:, 0:CH]), inc=VE)
    wrbf_done = VE.n
    p.fence("vector", VE)
    for c in range(CH):
        p.emit("vector", lambda e, c=c: e.tensor_copy(
            out=wrep[:, 128 * c:128 * (c + 1)],
            in_=wrbf[:, c:c + 1].to_broadcast([128, 128])), inc=VE)
    wrep_done = VE.n
    if DBG:
        p.dma("sync", dbg["dbg_wrbf"].ap(), wrbf[:, :], ST,
              waits=[(VE, wrbf_done)])
        p.dma("sync", dbg["dbg_wrsb"].ap(), wr_sb[:, 0:CH], ST,
              waits=[(VE, wrbf_done)])
    p.wait("tensor", Ls, ld_small)
    p.wait("tensor", VE, 3)
    p.emit("tensor", lambda e: e.matmul(
        arena[:, 2048:2049], ones1[:, :], br_sb[0:1, 0:1],
        start=True, stop=True), inc=PE)
    br_mm = PE.n
    p.wait("vector", PE, br_mm)
    p.emit("vector", lambda e: e.tensor_copy(out=cpool[:, 5:6],
                                             in_=arena[:, 2048:2049]), inc=VE)
    br128_done = VE.n

    for og in range(6):
        mod_evac_prev = st.get("mod_evac", 0)
        for i in range(CH):
            slot = i % 2
            p.dma("sync", fstage[:, slot * D:(slot + 1) * D],
                  t_wmod.ap()[128 * i:128 * (i + 1),
                              1024 * og:1024 * (og + 1)],
                  Ls, waits=([st["fstage_rel"][slot]]
                             if st["fstage_rel"][slot] else []))
            ldv = Ls.n
            p.wait("tensor", Ls, ldv)
            if i == 0:
                p.wait("tensor", SE, silc_done)
                p.wait("tensor", VE, mod_evac_prev)
                p.wait("tensor", VE, br128_done)
            for oi in range(CH):
                p.emit("tensor", lambda e, sl=slot, oi=oi, i=i: e.matmul(
                    pvw(512 * oi, 2),
                    fstage[:, sl * D + 128 * oi:sl * D + 128 * (oi + 1)],
                    silc[:, 2 * i:2 * i + 2],
                    start=(i == 0), stop=(i == CH - 1)),
                    inc=(PE if oi == CH - 1 else None))
            st["fstage_rel"][slot] = (PE, PE.n)
        mm_done = PE.n
        p.wait("vector", PE, mm_done)
        for oi in range(CH):
            oc = og * CH + oi
            p.emit("vector", lambda e, oi=oi, oc=oc: e.tensor_copy(
                out=modv[:, 2 * oc:2 * oc + 2], in_=pvw(512 * oi, 2)),
                inc=(VE if oi == CH - 1 else None))
        st["mod_evac"] = VE.n
    mod_done = VE.n

    def mcol(slot, c, b):
        i0 = 16 * slot + 2 * c + b
        return modv[:, i0:i0 + 1]

    p.wait("vector", VE, mod_done)
    p.fence("vector", VE)
    p.emit("vector", lambda e: e.tensor_scalar_add(
        modv[:, 16:32], modv[:, 16:32], 1.0), inc=VE)
    p.emit("vector", lambda e: e.tensor_scalar_add(
        modv[:, 64:80], modv[:, 64:80], 1.0), inc=VE)
    for sl_ in (2, 5):
        cs = slice(16 * sl_, 16 * sl_ + 16)
        p.fence("vector", VE)
        p.emit("vector", lambda e, cs=cs: e.tensor_scalar_max(
            modv[:, 96:112], modv[:, cs], 0.25), inc=VE)
        p.emit("vector", lambda e, cs=cs: e.tensor_scalar_min(
            modv[:, 112:128], modv[:, cs], -0.25), inc=VE)
        p.emit("vector", lambda e, cs=cs: e.tensor_scalar(
            modv[:, 128:144], modv[:, cs], 0.0, None, ALU.is_ge), inc=VE)
        p.fence("vector", VE)
        p.emit("vector", lambda e: e.tensor_sub(
            modv[:, 96:112], modv[:, 96:112], modv[:, 112:128]), inc=VE)
        p.fence("vector", VE)
        p.emit("vector", lambda e: e.tensor_mul(
            modv[:, 96:112], modv[:, 96:112], modv[:, 128:144]), inc=VE)
        p.fence("vector", VE)
        p.emit("vector", lambda e, cs=cs: e.tensor_add(
            modv[:, cs], modv[:, 112:128], modv[:, 96:112]), inc=VE)
    modv_done = VE.n
    if DBG:
        p.dma("sync", dbg["dbg_modv"].ap(), modv[:, 0:96], ST,
              waits=[(VE, modv_done)])

    def stage_weight(hbm_ap, slot):
        p.dma("sync", fstage[:, slot * D:(slot + 1) * D], hbm_ap, Ls,
              waits=([st["fstage_rel"][slot]]
                     if st["fstage_rel"][slot] else []))
        ldv = Ls.n
        p.wait("vector", Ls, ldv)
        p.wait("vector", PE, st["fbf_rel"][slot])
        p.emit("vector", lambda e, sl=slot: e.tensor_copy(
            out=fbf[:, sl * D:(sl + 1) * D],
            in_=fstage[:, sl * D:(sl + 1) * D]), inc=VE)
        st["fstage_rel"][slot] = (VE, VE.n)
        return VE.n

    def cm_ap(t, oc):
        base = t.ap()[oc]
        return bass.AP(base.tensor, base.offset,
                       [[128, 128], [128 * 128, 8], [1, 128]])

    def emit_ln(b, scale_slot, shift_slot, ready):
        p.wait("tensor", SE, SE.n)
        p.wait("tensor", VE, VE.n)
        stats_prev = 0
        for c in range(CH):
            sem_c, val_c = ready[c]
            p.wait("vector", sem_c, val_c)
            p.wait("vector", PE, stats_prev)
            p.emit("vector", lambda e, c=c: e.tensor_copy(
                out=gbf[:, 0:S], in_=x_sb[:, c * S:(c + 1) * S]), inc=VE)
            xbf_v = VE.n
            p.wait("tensor", VE, xbf_v)
            for t4 in range(4):
                p.emit("tensor", lambda e, c=c, t4=t4: e.matmul(
                    pvw(512 * t4, 512, 1),
                    onebf[:, :], gbf[:, 512 * t4:512 * (t4 + 1)],
                    start=(c == 0), stop=(c == CH - 1)),
                    inc=(PE if t4 == 3 else None))
            sx_mm = PE.n
            p.wait("scalar", sem_c, val_c)
            p.wait("scalar", PE, sx_mm)
            p.emit("scalar", lambda e, c=c: e.activation(
                out=gbf[:, 0:S], in_=x_sb[:, c * S:(c + 1) * S],
                func=AF.Square), inc=SE)
            x2_v = SE.n
            p.wait("tensor", SE, x2_v)
            for t4 in range(4):
                p.emit("tensor", lambda e, c=c, t4=t4: e.matmul(
                    pvw(2048 + 512 * t4, 512, 1),
                    onebf[:, :], gbf[:, 512 * t4:512 * (t4 + 1)],
                    start=(c == 0), stop=(c == CH - 1)),
                    inc=(PE if t4 == 3 else None))
            stats_prev = PE.n
        stats_done = PE.n
        r1 = gf32[0:1, 0:S]
        r2 = gf32[0:1, S:2 * S]
        p.wait("vector", PE, stats_done)
        p.emit("vector", lambda e: e.tensor_scalar_mul(
            r1, pvw(0, S, 1), 1.0 / D), inc=VE)
        p.emit("vector", lambda e: e.tensor_scalar_mul(
            r2, pvw(2048, S, 1), 1.0 / D), inc=VE)
        p.fence("vector", VE)
        rows_v = VE.n
        p.wait("tensor", VE, rows_v)
        for t4 in range(4):
            p.emit("tensor", lambda e, t4=t4: e.matmul(
                pvw(512 * t4, 512), ones1[:, :],
                r1[:, 512 * t4:512 * (t4 + 1)],
                start=True, stop=True), inc=(PE if t4 == 3 else None))
        mub_done = PE.n
        p.wait("scalar", PE, mub_done)
        p.emit("scalar", lambda e: e.activation(out=r1, in_=r1,
                                                func=AF.Square), inc=SE)
        p.wait("vector", SE, SE.n)
        p.emit("vector", lambda e: e.tensor_sub(r2, r2, r1), inc=VE)
        p.wait("scalar", VE, VE.n)
        p.wait("scalar", VE, cpool_done)
        p.emit("scalar", lambda e: e.activation(out=r2, in_=r2, func=AF.Sqrt,
                                                bias=cpool[0:1, 0:1]), inc=SE)
        p.wait("vector", SE, SE.n)
        p.emit("vector", lambda e: e.reciprocal(out=r2, in_=r2), inc=VE)
        rstd_v = VE.n
        p.wait("tensor", VE, rstd_v)
        for t4 in range(4):
            p.emit("tensor", lambda e, t4=t4: e.matmul(
                pvw(2048 + 512 * t4, 512), ones1[:, :],
                r2[:, 512 * t4:512 * (t4 + 1)],
                start=True, stop=True), inc=(PE if t4 == 3 else None))
        rstdb_done = PE.n
        tsl = gf32[:, S:2 * S]
        p.wait("vector", PE, rstdb_done)
        p.wait("vector", VE, modv_done)
        p.wait("scalar", VE, modv_done)
        h_done = []
        for c in range(CH):
            p.fence("vector", VE)
            p.emit("vector", lambda e, c=c: e.tensor_sub(
                tsl, x_sb[:, c * S:(c + 1) * S], pvw(0, S)), inc=VE)
            p.fence("vector", VE)
            p.emit("vector", lambda e, c=c: e.tensor_mul(
                tsl, tsl, pvw(2048, S)), inc=VE)
            p.fence("vector", VE)
            p.emit("vector", lambda e, c=c: e.tensor_scalar(
                h_sb[:, c * S:(c + 1) * S], tsl,
                mcol(scale_slot, c, b), mcol(shift_slot, c, b),
                ALU.mult, ALU.add), inc=VE)
            h_done.append(VE.n)
        return h_done

    seq_store_done = 0
    for b in range(NSEQ):
        if b == 0:
            x_ld = x_ld0
        else:
            x_ld = []
            for c in range(CH):
                p.dma("scalar", x_sb[:, c * S:(c + 1) * S],
                      t_x.ap()[b, 128 * c:128 * (c + 1), :], Lc,
                      waits=[(ST, seq_store_done)])
                x_ld.append((Lc, Lc.n))

        h_done = emit_ln(b, 1, 0, x_ld)
        h_all = h_done[-1]
        if DBG:
            for c in range(CH):
                p.dma("sync", dbg["dbg_h"].ap()[b, c],
                      h_sb[:, c * S:(c + 1) * S], ST,
                      waits=[(VE, h_done[c])])

        out_norm = {}
        for half in range(2):
            QA = QAOF[half]
            hc0 = 4 * half
            for wt, toff, scale in ((t_wq, QA, 0.125), (t_wk, KOFF, 1.0)):
                for oci in range(4):
                    oc = hc0 + oci
                    slot = oci % 2
                    wready = stage_weight(cm_ap(wt, oc), slot)
                    for tc in range(4):
                        bank = tc
                        key = ("qk_ev", bank)
                        p.wait("tensor", SE, st.get(key, 0))
                        p.wait("tensor", VE, wready)
                        p.wait("tensor", VE, h_all)
                        for i in range(CH):
                            p.emit("tensor", lambda e, i=i, sl=slot, tc=tc,
                                   bank=bank: e.matmul(
                                       pvw(512 * bank, 512),
                                       fbf[:, sl * D + 128 * i:
                                           sl * D + 128 * (i + 1)],
                                       h_sb[:, i * S + 512 * tc:
                                            i * S + 512 * (tc + 1)],
                                       start=(i == 0), stop=(i == CH - 1)),
                                   inc=(PE if i == CH - 1 else None))
                        st["fbf_rel"][slot] = PE.n
                        mmv = PE.n
                        p.wait("scalar", PE, mmv)
                        p.wait("scalar", VE, cpool_done)
                        if scale == 1.0:
                            p.emit("scalar", lambda e, oci=oci, tc=tc,
                                   toff=toff, bank=bank: e.activation(
                                       out=big[:, toff + 2048 * oci + 512 * tc:
                                               toff + 2048 * oci +
                                               512 * (tc + 1)],
                                       in_=pvw(512 * bank, 512),
                                       func=AF.Copy), inc=SE)
                        else:
                            p.emit("scalar", lambda e, oci=oci, tc=tc,
                                   toff=toff, bank=bank: e.activation(
                                       out=big[:, toff + 2048 * oci + 512 * tc:
                                               toff + 2048 * oci +
                                               512 * (tc + 1)],
                                       in_=pvw(512 * bank, 512),
                                       func=AF.Copy, scale=cpool[:, 2:3]),
                                   inc=SE)
                        st[key] = SE.n
            qk_done = SE.n

            wv_ready = []
            for i in range(CH):
                slot = i % 2
                p.dma("sync", fstage[:, slot * D:slot * D + 512],
                      t_wv.ap()[128 * i:128 * (i + 1),
                                512 * half:512 * (half + 1)],
                      Ls, waits=([st["fstage_rel"][slot]]
                                 if st["fstage_rel"][slot] else []))
                ldv = Ls.n
                p.wait("vector", Ls, ldv)
                p.wait("vector", GP, st.get("mask_user", 0))
                p.wait("vector", PE, st.get("w2_user", 0))
                p.emit("vector", lambda e, i=i, sl=slot: e.tensor_copy(
                    out=w2res[:, 512 * i:512 * (i + 1)],
                    in_=fstage[:, sl * D:sl * D + 512]), inc=VE)
                st["fstage_rel"][slot] = (VE, VE.n)
                wv_ready.append(VE.n)
            p.wait("gpsimd", PE, PE.n)
            p.wait("gpsimd", SE, SE.n)
            p.emit("gpsimd", lambda e: e.memset(vview[:, :, :, 64:65], 1.0),
                   inc=GP)
            vones = GP.n
            for tt in range(16):
                pb = tt % 2
                key = ("v_ev", pb)
                p.wait("tensor", SE, st.get(key, 0))
                p.wait("tensor", VE, h_all)
                for i in range(CH):
                    p.wait("tensor", VE, wv_ready[i])
                    p.emit("tensor", lambda e, i=i, tt=tt, pb=pb: e.matmul(
                        pvw(2048 + 512 * pb, 512),
                        h_sb[:, i * S + 128 * tt:i * S + 128 * (tt + 1)],
                        w2res[:, 512 * i:512 * (i + 1)],
                        start=(i == 0), stop=(i == CH - 1)),
                        inc=(PE if i == CH - 1 else None))
                mmv = PE.n
                st["v_stage_mm"] = mmv
                p.wait("scalar", PE, mmv)
                p.wait("scalar", GP, vones)
                psrc = arena.ap()[:, 2048 + 512 * pb:2048 + 512 * (pb + 1)] \
                    .rearrange("p (h x) -> p h x", h=8)
                p.emit("scalar", lambda e, tt=tt, psrc=psrc: e.activation(
                    out=vview[:, :, tt:tt + 1, 0:64], in_=psrc,
                    func=AF.Copy), inc=SE)
                st[key] = SE.n
            v_done = SE.n

            if DBG:
                for c in range(4):
                    p.dma("sync", dbg["dbg_q"].ap()[b, hc0 + c],
                          big[:, QA + 2048 * c:QA + 2048 * (c + 1)], ST,
                          waits=[(SE, qk_done)])
                    p.dma("sync", dbg["dbg_k"].ap()[b, hc0 + c],
                          big[:, KOFF + 2048 * c:KOFF + 2048 * (c + 1)], ST,
                          waits=[(SE, qk_done)])
                p.dma("sync", dbg["dbg_v"].ap()[b, half],
                      big[:, VOFF:VOFF + 8 * 1040], ST,
                      waits=[(SE, v_done)])

            t_mask = t_ms if half == 0 else t_mf
            s_rel = [0, 0, 0, 0]
            p_rel = [0, 0, 0, 0]
            pv_rel = [0, 0]
            stg_rel = [0, 0]
            for qc in range(4):
                kmax = 4 * qc + 4
                for k0 in range(0, kmax, 4):
                    kn = min(4, kmax - k0)
                    mask_in = t_mask.ap()[128 * k0:128 * (k0 + kn),
                                          512 * qc:512 * (qc + 1)] \
                        .rearrange("(k p) q -> p k q", p=128)
                    mask_out = masku8[:, k0 * QC:(k0 + kn) * QC].rearrange(
                        "p (k q) -> p k q", q=QC)
                    p.dma("scalar", mask_out, mask_in, Lc,
                          waits=[(GP, st.get("mask_user", 0)),
                                 (PE, st.get("v_stage_mm", 0)),
                                 (VE, st.get("epi_t3", 0)),
                                 (SE, v_done)])
                mask_ld = Lc.n
                for pr in range(4):
                    for kt in range(kmax):
                        sa = 2 * (kt % 2)
                        sb_ = sa + 1
                        p.wait("tensor", SE, max(s_rel[sa], s_rel[sb_]))
                        p.wait("tensor", SE, qk_done)
                        p.emit("tensor", lambda e, c=pr, kt=kt, qc=qc,
                               sa=sa, QA=QA: e.matmul(
                                   pvw(512 * sa, 512),
                                   big[0:64, KOFF + 2048 * c + 128 * kt:
                                       KOFF + 2048 * c + 128 * (kt + 1)],
                                   big[0:64, QA + 2048 * c + 512 * qc:
                                       QA + 2048 * c + 512 * (qc + 1)],
                                   start=True, stop=True))
                        p.emit("tensor", lambda e, c=pr, kt=kt, qc=qc,
                               sb_=sb_, QA=QA: e.matmul(
                                   pvw(512 * sb_, 512),
                                   big[64:128, KOFF + 2048 * c + 128 * kt:
                                       KOFF + 2048 * c + 128 * (kt + 1)],
                                   big[64:128, QA + 2048 * c + 512 * qc:
                                       QA + 2048 * c + 512 * (qc + 1)],
                                   start=True, stop=True,
                                   tile_position=(64, 0)), inc=PE)
                        smm = PE.n
                        for hh in range(2):
                            psl = sa + hh
                            p.wait("scalar", PE, max(smm, p_rel[psl]))
                            p.wait("scalar", VE, cpool_done)
                            p.emit("scalar", lambda e, psl=psl,
                                   src=(sa + hh): e.activation(
                                       out=gbf[:, 512 * psl:512 * (psl + 1)],
                                       in_=pvw(512 * src, 512),
                                       func=AF.Exp, bias=cpool[:, 1:2]),
                                   inc=SE)
                        s_rel[sa] = SE.n
                        s_rel[sb_] = SE.n
                        expv = SE.n
                        p.wait("gpsimd", SE, expv)
                        p.wait("gpsimd", Lc, mask_ld)
                        for hh in range(2):
                            psl = sa + hh
                            p.emit("gpsimd", lambda e, psl=psl, kt=kt:
                                   e.tensor_mul(
                                       gbf[:, 512 * psl:512 * (psl + 1)],
                                       gbf[:, 512 * psl:512 * (psl + 1)],
                                       masku8[:, QC * kt:QC * (kt + 1)]),
                                   inc=GP)
                        maskv = GP.n
                        st["mask_user"] = maskv
                        p.wait("tensor", GP, maskv)
                        p.wait("tensor", SE, v_done)
                        for hh in range(2):
                            vh = 2 * pr + hh
                            psl = sa + hh
                            p.wait("tensor", VE, pv_rel[hh])
                            p.emit("tensor", lambda e, vh=vh, kt=kt, psl=psl,
                                   hh=hh, kmax=kmax: e.matmul(
                                       pvw(3072 + 512 * hh, 512, 65),
                                       big[:, VOFF + 1040 * vh + 65 * kt:
                                           VOFF + 1040 * vh + 65 * (kt + 1)],
                                       gbf[:, 512 * psl:512 * (psl + 1)],
                                       start=(kt == 0), stop=(kt == kmax - 1)),
                                   inc=(PE if hh == 1 else None))
                            p_rel[psl] = PE.n
                    st["v_user"] = PE.n
                    pvmm = PE.n
                    for hh in range(2):
                        p.wait("vector", PE, pvmm)
                        p.wait("vector", Lc, stg_rel[hh])
                        p.emit("vector", lambda e, hh=hh:
                               e.tensor_copy(
                                   out=stg[:, QC * hh:QC * (hh + 1)],
                                   in_=pvw(3072 + 512 * hh, 512, 65)),
                               inc=VE)
                        pv_rel[hh] = VE.n
                    vpad()
                    evv = VE.n
                    for hh in range(2):
                        vh = 2 * pr + hh
                        oa = QA + 2048 * pr + QC * qc
                        p.dma("scalar",
                              big[64 * hh:64 * (hh + 1), oa:oa + QC],
                              stg[0:64, QC * hh:QC * (hh + 1)], Lc,
                              waits=[(VE, evv)])
                        p.dma("scalar",
                              rzbf[vh:vh + 1, QC * qc:QC * (qc + 1)],
                              stg[64:65, QC * hh:QC * (hh + 1)], Lc,
                              waits=[(VE, evv),
                                     (PE, st.get("rz_mm_done", 0))])
                        stg_rel[hh] = Lc.n
            zdma = Lc.n
            if DBG:
                p.dma("sync", dbg["dbg_z"].ap()[b, half], rzbf[:, :], ST,
                      waits=[(Lc, zdma)])
                p.wait("vector", ST, ST.n)
            p.wait("vector", Lc, zdma)
            for _pd in range(4):
                vpad()

            def _recip(e):
                with nc.allow_low_precision(reason="1/Z in bf16 is fine"):
                    return e.reciprocal(out=rzbf[0:8, :], in_=rzbf[0:8, :])
            p.emit("vector", _recip, inc=VE)
            rz_done = VE.n
            if DBG:
                p.dma("sync", dbg["dbg_rz"].ap()[b, half], rzbf[:, :], ST,
                      waits=[(VE, rz_done)])
            for ci in range(4):
                for t4 in range(4):
                    pb2 = t4 % 2
                    p.wait("tensor", VE, max(rz_done,
                                             st.get(("rz_ev", pb2), 0)))
                    p.wait("tensor", Ls, consts_lg)
                    p.emit("tensor", lambda e, ci=ci, t4=t4, pb2=pb2:
                           e.matmul(
                               pvw(2048 + 512 * pb2, 512),
                               selmat[:, 128 * ci:128 * (ci + 1)],
                               rzbf[:, 512 * t4:512 * (t4 + 1)],
                               start=True, stop=True), inc=PE)
                    mmz = PE.n
                    p.wait("vector", PE, mmz)
                    p.emit("vector", lambda e, ci=ci, t4=t4, pb2=pb2, QA=QA:
                           e.tensor_mul(
                               big[:, QA + 2048 * ci + 512 * t4:
                                   QA + 2048 * ci + 512 * (t4 + 1)],
                               big[:, QA + 2048 * ci + 512 * t4:
                                   QA + 2048 * ci + 512 * (t4 + 1)],
                               pvw(2048 + 512 * pb2, 512)), inc=VE)
                    st[("rz_ev", pb2)] = VE.n
                out_norm[hc0 + ci] = VE.n
            st["rz_mm_done"] = PE.n
            if DBG:
                for ci in range(4):
                    p.dma("sync", dbg["dbg_attn"].ap()[b, hc0 + ci],
                          big[:, QA + 2048 * ci:QA + 2048 * (ci + 1)], ST,
                          waits=[(VE, out_norm[hc0 + ci])])

        p.wait("tensor", SE, SE.n)
        SAFE = bool(int(os.environ.get("MK_SAFE_WO", "0")))
        for dc in range(CH):
            slot = dc % 2
            wready = stage_weight(cm_ap(t_wo, dc), slot)
            for tc in range(4):
                bank = tc
                if SAFE:
                    p.wait("tensor", VE, VE.n)
                    p.wait("tensor", SE, SE.n)
                    p.wait("tensor", Ls, Ls.n)
                p.wait("tensor", VE, wready)
                p.wait("tensor", VE, st.get(("wo_ev", bank), 0))
                for fi in range(CH):
                    p.wait("tensor", VE, out_norm[fi])
                    src = QAOF[fi // 4] + 2048 * (fi % 4)
                    p.emit("tensor", lambda e, fi=fi, tc=tc, sl=slot,
                           bank=bank, src=src: e.matmul(
                               pvw(512 * bank, 512),
                               fbf[:, sl * D + 128 * fi:
                                   sl * D + 128 * (fi + 1)],
                               big[:, src + 512 * tc:src + 512 * (tc + 1)],
                               start=(fi == 0), stop=(fi == CH - 1)),
                           inc=(PE if fi == CH - 1 else None))
                st["fbf_rel"][slot] = PE.n
                mmv = PE.n
                p.wait("vector", PE, mmv)
                p.wait("vector", VE, modv_done)
                p.emit("vector", lambda e, dc=dc, tc=tc, bank=bank, b=b:
                       e.scalar_tensor_tensor(
                           out=x_sb[:, dc * S + 512 * tc:
                                    dc * S + 512 * (tc + 1)],
                           in0=pvw(512 * bank, 512),
                           scalar=mcol(2, dc, b),
                           in1=x_sb[:, dc * S + 512 * tc:
                                    dc * S + 512 * (tc + 1)],
                           op0=ALU.mult, op1=ALU.add), inc=VE)
                st[("wo_ev", bank)] = VE.n
                if SAFE:
                    p.wait("vector", PE, PE.n)
        xattn_done = VE.n
        if DBG:
            for c in range(CH):
                p.dma("sync", dbg["dbg_xa"].ap()[b, c],
                      x_sb[:, c * S:(c + 1) * S], ST,
                      waits=[(VE, xattn_done)])

        h2_done = emit_ln(b, 4, 3, [(VE, xattn_done)] * CH)
        h2_all = h2_done[-1]
        if DBG:
            for c in range(CH):
                p.dma("sync", dbg["dbg_h2"].ap()[b, c],
                      h_sb[:, c * S:(c + 1) * S], ST,
                      waits=[(VE, h2_done[c])])

        if bool(int(os.environ.get("MK_RANK_BARRIER", "0"))):
            for eng in ENGINES:
                for sm in (PE, VE, SE, GP, Ls, Lc, ST):
                    p.wait(eng, sm, sm.n)
        p.wait("tensor", VE, wrep_done)
        p.wait("tensor", VE, h2_all)
        for c in range(CH):
            for t4 in range(4):
                p.emit("tensor", lambda e, c=c, t4=t4: e.matmul(
                    pvw(2048 + 512 * t4, 512),
                    wrep[:, 128 * c:128 * (c + 1)],
                    h_sb[:, c * S + 512 * t4:c * S + 512 * (t4 + 1)],
                    start=(c == 0), stop=(c == CH - 1)),
                    inc=(PE if (c == CH - 1 and t4 == 3) else None))
        wbc_mm = PE.n
        p.wait("tensor", VE, h2_all)
        p.wait("tensor", VE, wrbf_done)
        for tt in range(16):
            for i in range(CH):
                p.emit("tensor", lambda e, tt=tt, i=i: e.matmul(
                    arena[:, tt:tt + 1],
                    h_sb[:, i * S + 128 * tt:i * S + 128 * (tt + 1)],
                    wrbf[:, i:i + 1],
                    start=(tt == 0 and i == 0),
                    stop=(tt == 15 and i == CH - 1)),
                    inc=(PE if (i == CH - 1) else None))
        wcol_mm = PE.n
        p.wait("vector", PE, wcol_mm)
        p.emit("vector", lambda e: e.tensor_copy(
            out=wcol[:, 0:16], in_=arena[:, 0:16]), inc=VE)
        vpad()
        wcol_done = VE.n
        p.wait("vector", VE, wcol_done)
        p.wait("vector", PE, wbc_mm)
        p.wait("vector", GP, st.get("mask_user", 0))
        cslot_rel = [0, 0]
        for sc in range(16):
            sl = sc % 2
            p.wait("vector", PE, cslot_rel[sl])
            p.emit("vector", lambda e, sc=sc, sl=sl: e.tensor_scalar(
                w2res[:, 2048 * sl:2048 * (sl + 1)], pvw(2048, S),
                wcol[:, sc:sc + 1], None, ALU.is_le), inc=VE)
            cmpv = VE.n
            p.wait("tensor", VE, cmpv)
            p.wait("tensor", Ls, consts_lg)
            for t4 in range(4):
                p.emit("tensor", lambda e, sc=sc, sl=sl, t4=t4: e.matmul(
                    pvw(512 * t4, 512), id128[:, :],
                    w2res[:, 2048 * sl + 512 * t4:2048 * sl + 512 * (t4 + 1)],
                    start=(sc == 0), stop=(sc == 15)),
                    inc=(PE if t4 == 3 else None))
            cslot_rel[sl] = PE.n
        rank_mm = PE.n
        st["w2_user"] = rank_mm
        p.wait("vector", PE, rank_mm)
        p.emit("vector", lambda e: e.tensor_copy(
            out=gf32[:, 0:S], in_=pvw(0, S)), inc=VE)
        vpad()
        acc_done = VE.n
        if DBG:
            p.dma("sync", dbg["dbg_wcol"].ap()[b], wcol[:, :], ST,
                  waits=[(VE, wcol_done)])
            p.emit("vector", lambda e: e.tensor_copy(out=cmpb,
                                                     in_=pvw(2048, S)),
                   inc=VE)
            vpad()
            p.dma("sync", dbg["dbg_wbc"].ap()[b], cmpb, ST,
                  waits=[(VE, VE.n)])
            p.wait("tensor", ST, ST.n)
        p.wait("tensor", VE, acc_done)
        for t4 in range(4):
            p.emit("tensor", lambda e, t4=t4: e.matmul(
                pvw(2048 + 512 * t4, 512, 1), one32[:, :],
                gf32[:, 512 * t4:512 * (t4 + 1)],
                start=True, stop=True), inc=(PE if t4 == 3 else None))
        ct_mm = PE.n
        selrow = gf32[0:1, S:2 * S]
        p.wait("vector", PE, ct_mm)
        p.emit("vector", lambda e: e.tensor_scalar(
            selrow, pvw(2048, S, 1), 1023.5, None, ALU.is_le), inc=VE)
        sel_done = VE.n
        if DBG:
            p.dma("scalar", dbg["dbg_sel"].ap()[b], selrow, ST,
                  waits=[(VE, sel_done)])
        p.wait("tensor", VE, sel_done)
        for t4 in range(4):
            p.emit("tensor", lambda e, t4=t4: e.matmul(
                pvw(2048 + 512 * t4, 512), ones1[:, :],
                selrow[:, 512 * t4:512 * (t4 + 1)], start=True, stop=True),
                inc=(PE if t4 == 3 else None))
        selb_mm = PE.n
        p.wait("vector", PE, selb_mm)
        p.emit("vector", lambda e: e.tensor_copy(
            out=selu8[:, 0:S], in_=pvw(2048, S)), inc=VE)
        selbf_done = VE.n
        if DBG:
            p.dma("sync", dbg["dbg_selu8"].ap()[b], selu8[:, 0:S], ST,
                  waits=[(VE, selbf_done)])
            p.dma("scalar", dbg["dbg_acc"].ap()[b], gf32[:, 0:S], ST,
                  waits=[(VE, acc_done)])

        for tr in range(2):
            TB = 1024 * tr
            for oc in range(32):
                slot = oc % 2
                wready = stage_weight(cm_ap(t_w1, oc), slot)
                pb = oc % 2
                key = ("w1_ev", pb)
                p.wait("tensor", SE, st.get(key, 0))
                p.wait("tensor", VE, wready)
                p.wait("tensor", VE, h2_all)
                for i in range(CH):
                    for nh in range(2):
                        p.emit("tensor", lambda e, i=i, sl=slot, nh=nh,
                               pb=pb, TB=TB: e.matmul(
                                   pvw(1024 * pb + 512 * nh, 512),
                                   fbf[:, sl * D + 128 * i:
                                       sl * D + 128 * (i + 1)],
                                   h_sb[:, i * S + TB + 512 * nh:
                                        i * S + TB + 512 * (nh + 1)],
                                   start=(i == 0), stop=(i == CH - 1)),
                               inc=(PE if (i == CH - 1 and nh == 1)
                                    else None))
                st["fbf_rel"][slot] = PE.n
                w1mm = PE.n
                p.wait("scalar", PE, w1mm)
                p.wait("scalar", Ls, ld_small)
                p.emit("scalar", lambda e, oc=oc, pb=pb: e.activation(
                    out=big[:, 1024 * oc:1024 * (oc + 1)],
                    in_=pvw(1024 * pb, 1024), func=AF.Gelu,
                    bias=b1_sb[:, oc:oc + 1]), inc=SE)
                st[key] = SE.n
            hid_done = SE.n

            for dhalf in range(2):
                p.wait("tensor", SE, hid_done)
                for dd in range(4):
                    p.wait("tensor", VE, st.get(("w2_ev", dd), 0))
                for hc in range(32):
                    slot = hc % 2
                    wready = stage_weight(t_w2.ap()[128 * hc:128 * (hc + 1),
                                                    1024 * 0:1024 * 1], slot)
                    p.wait("tensor", VE, wready)
                    for dd in range(4):
                        dcp = 4 * dhalf + dd
                        for nh in range(2):
                            p.emit("tensor", lambda e, hc=hc, dcp=dcp,
                                   nh=nh, dd=dd, sl=slot: e.matmul(
                                       pvw(1024 * dd + 512 * nh, 512),
                                       fbf[:, sl * D +
                                           128 * (dcp % CH):
                                           sl * D + 128 * (dcp % CH) + 128],
                                       big[:, 1024 * hc + 512 * nh:
                                           1024 * hc + 512 * (nh + 1)],
                                       start=(hc == 0), stop=(hc == 31)),
                                   inc=(PE if (dd == 3 and nh == 1)
                                        else None))
                    st["fbf_rel"][slot] = PE.n
                w2mm = PE.n
                t1 = gf32[:, S:S + 1024]
                t3 = gf32[:, S + 1024:S + 2048]
                for dd in range(4):
                    dcp = 4 * dhalf + dd
                    p.wait("vector", PE, w2mm)
                    p.fence("vector", VE)
                    p.emit("vector", lambda e, dcp=dcp, dd=dd:
                           e.tensor_scalar_add(
                               t1, pvw(1024 * dd, 1024),
                               b2_sb[:, dcp:dcp + 1]), inc=VE)
                    st[("w2_ev", dd)] = VE.n
                    p.fence("vector", VE)
                    p.emit("vector", lambda e, dcp=dcp, TB=TB: e.select(
                        out=t3, mask=selu8[:, TB:TB + 1024], on_true=t1,
                        on_false=h_sb[:, dcp * S + TB:dcp * S + TB + 1024]),
                        inc=VE)
                    p.fence("vector", VE)
                    p.emit("vector", lambda e, dcp=dcp, TB=TB, b=b:
                           e.scalar_tensor_tensor(
                               out=x_sb[:, dcp * S + TB:
                                        dcp * S + TB + 1024],
                               in0=t3, scalar=mcol(5, dcp, b),
                               in1=x_sb[:, dcp * S + TB:
                                        dcp * S + TB + 1024],
                               op0=ALU.mult, op1=ALU.add), inc=VE)
                    st["epi_t3"] = VE.n
                    vpad()
                    xfin = VE.n
                    p.dma("sync", t_y.ap()[b, dcp][:, TB:TB + 1024],
                          x_sb[:, dcp * S + TB:dcp * S + TB + 1024], ST,
                          waits=[(VE, xfin)])
        seq_store_done = ST.n

    p.wait("sync", ST, ST.n)
    p.wait("sync", Lc, Lc.n)
    p.wait("sync", Lg, Lg.n)

    with nc.Block() as block:
        for eng_name in ENGINES:
            def make(nm):
                def runner(eng):
                    for fn in p.streams[nm]:
                        fn(eng)
                return runner
            getattr(block, eng_name)(make(eng_name))
    for cm_ in ctxs:
        cm_.__exit__(None, None, None)
    return nc


def kernel(**inputs):
    lay = host_layout(inputs)
    nc = build_program()
    in_maps = []
    for core in range(8):
        m = {}
        for k, v in lay.items():
            if k == "xT":
                m[k] = np.ascontiguousarray(v[2 * core:2 * core + 2])
            elif k == "cT":
                m[k] = np.ascontiguousarray(v[:, 2 * core:2 * core + 2])
            else:
                m[k] = v
        in_maps.append(m)
    res = run_bass_kernel_spmd(nc, in_maps, core_ids=list(range(8)))
    outs = []
    for core in range(8):
        y = np.asarray(res.results[core]["y"])
        outs.append(y.transpose(0, 3, 1, 2).reshape(NSEQ, S, D))
    return np.concatenate(outs, axis=0).astype(np.float32)
